# version 04 (final)
# baseline (speedup 1.0000x reference)
import math
from contextlib import ExitStack

import numpy as np
import concourse.bass as bass
import concourse.mybir as mybir
from concourse.bass_utils import run_bass_kernel_spmd

F32 = mybir.dt.float32
BF16 = mybir.dt.bfloat16
AF = mybir.ActivationFunctionType
ALU = mybir.AluOpType
AX = mybir.AxisListType

D = 2048
DC = 16
H = 1408
HC = 11
NE = 16
NH = 16
HD = 128
ISCALE = float(1.0 / math.sqrt(HD))
NEG_BIG = -1e30

SEQ = 8192
B = 2
NCORES = 8
TTOK = (B * SEQ) // NCORES
TP = 1024


class Sem:
    def __init__(self, ctx, nc, name):
        self.h = ctx.enter_context(nc.semaphore(name))
        self.n = 0
        self._waited = {}

    def inc(self, instr, k=1):
        instr.then_inc(self.h, k)
        self.n += k
        return self.n

    def wait(self, eng, val=None):
        if val is None:
            val = self.n
        if val <= 0:
            return
        key = id(eng)
        if self._waited.get(key, -1) >= val:
            return
        eng.wait_ge(self.h, val)
        self._waited[key] = val


class Prog:
    def __init__(self, nc, ctx):
        self.nc = nc
        self.ctx = ctx
        self.sems = {}
        self.prefix = ""
        self._arm_stack = []
        self._gen = {}

    def begin_arms(self):
        saved = {nm: (s.n, dict(s._waited)) for nm, s in self.sems.items()}
        self._arm_stack.append((saved, []))

    def end_arm(self):
        saved, finals = self._arm_stack[-1]
        finals.append({nm: (s.n, dict(s._waited))
                       for nm, s in self.sems.items()})
        for nm, s in self.sems.items():
            if nm in saved:
                s.n, s._waited = saved[nm][0], dict(saved[nm][1])
            else:
                s.n, s._waited = 0, {}

    def end_arms(self):
        saved, finals = self._arm_stack.pop()
        if not finals:
            return
        ref = finals[-1]
        drop = set()
        for nm in ref:
            for f2 in finals[:-1]:
                if f2.get(nm, (0,))[0] != ref[nm][0]:
                    drop.add(nm)
                    break
        for nm, (n2, w2) in ref.items():
            if nm in drop:
                del self.sems[nm]
                self._gen[nm] = self._gen.get(nm, 0) + 1
            else:
                s = self.sems[nm]
                s.n, s._waited = n2, dict(w2)

    def sem(self, name):
        name = self.prefix + name
        if name not in self.sems:
            gen = self._gen.get(name, 0)
            hw = name if gen == 0 else f"{name}__g{gen}"
            self.sems[name] = Sem(self.ctx, self.nc, hw)
        return self.sems[name]

    def sbuf(self, name, shape, dtype):
        return self.ctx.enter_context(self.nc.sbuf_tensor(name, shape, dtype))

    def psum(self, name, shape, dtype=F32):
        return self.ctx.enter_context(self.nc.psum_tensor(name, shape, dtype))

    def barrier(self):
        nc = self.nc
        engines = [nc.sync, nc.gpsimd, nc.tensor, nc.vector, nc.scalar]
        s = self.sem("s_bar")
        for e in engines:
            e.drain()
            s.inc(e.nop(), 1)
        for e in engines:
            e.wait_ge(s.h, s.n)


def emit_consts(p, need_sel=True, need_tri=False):
    nc = p.nc
    c = {}
    g = nc.gpsimd
    c["ones_col_bf"] = p.sbuf("ones_col_bf", [128, 1], BF16)
    c["ones_row_f"] = p.sbuf("ones_row_f", [1, 128], F32)
    c["eps"] = p.sbuf("eps_col", [128, 1], F32)
    g.memset(c["ones_col_bf"][:, :], 1.0)
    g.drain()
    g.memset(c["ones_row_f"][:, :], 1.0)
    g.drain()
    g.memset(c["eps"][:, :], 1e-5)
    g.drain()
    if need_tri:
        c["tri"] = p.sbuf("tri", [128, 128], BF16)
        g.memset(c["tri"][:, :], 0.0)
        g.drain()
        g.affine_select(out=c["tri"][:, :], in_=c["tri"][:, :],
                        compare_op=ALU.is_gt, fill=1.0,
                        base=0, pattern=[[-1, 128]], channel_multiplier=1)
        g.drain()
    if need_sel:
        c["sel"] = p.sbuf("sel", [16, NE * 128], F32)
        g.memset(c["sel"][:, :], 0.0)
        g.drain()
        g.affine_select(
            out=c["sel"][:, :].rearrange("p (e j) -> p e j", e=NE),
            in_=c["sel"][:, :].rearrange("p (e j) -> p e j", e=NE),
            compare_op=ALU.not_equal, fill=1.0,
            base=0, pattern=[[-1, NE], [0, 128]], channel_multiplier=1)
        g.drain()
        c["ident_f"] = p.sbuf("ident_f", [128, 128], F32)
        g.memset(c["ident_f"][:, :], 0.0)
        g.drain()
        g.affine_select(out=c["ident_f"][:, :], in_=c["ident_f"][:, :],
                        compare_op=ALU.not_equal, fill=1.0,
                        base=0, pattern=[[-1, 128]], channel_multiplier=1)
        g.drain()
    p.barrier()
    return c


class WeightRing:

    def __init__(self, p, name, slots):
        self.p = p
        self.nslots = len(slots)
        self.slots = slots
        self.s_ld = [p.sem(f"s_{name}_ld{i}") for i in range(self.nslots)]
        self.queue = []
        self.marks = []
        self.n_issued = 0
        self.n_used = 0

    def enqueue(self, src_ap):
        self.queue.append(src_ap)
        self.pump()

    def pump(self):
        g = self.p.nc.gpsimd
        while (self.n_issued < len(self.queue)
               and self.n_issued < len(self.marks) + self.nslots):
            if self.n_issued >= self.nslots:
                sem, val = self.marks[self.n_issued - self.nslots]
                sem.wait(g, val)
            src = self.queue[self.n_issued]
            si = self.n_issued % self.nslots
            slot = self.slots[si]
            nch = src.shape[1]
            self.s_ld[si].inc(g.dma_start(out=slot[:, 0:nch, :], in_=src), 16)
            self.n_issued += 1

    def use(self):
        t = self.p.nc.tensor
        self.n_used += 1
        assert self.n_used <= self.n_issued, "ring starved"
        si = (self.n_used - 1) % self.nslots
        t.wait_ge(self.s_ld[si].h, self.s_ld[si].n)
        return self.slots[si]

    def mark(self, sem, val):
        self.marks.append((sem, val))
        self.pump()


def host_pieces(w, n_in_chunks, n_out_chunks):
    return np.ascontiguousarray(
        w.reshape(n_in_chunks, 128, n_out_chunks, 128).transpose(2, 1, 0, 3))


class FFNBufs:

    def __init__(self, p, Tp, psw=None):
        assert Tp % 512 == 0
        psw = psw or Tp
        self.Tp_alloc = Tp
        self.psw = psw
        self.xln = p.sbuf("xln", [128, DC, Tp], BF16)
        self.acc = p.sbuf("acc", [128, DC, Tp], F32)
        self.h3 = p.sbuf("h3", [128, HC, Tp], BF16)
        self.probs_fm = p.sbuf("probs_fm", [16, Tp], F32)
        self.probs_rep = p.sbuf("probs_rep", [128, Tp], F32)
        self.fA = p.sbuf("fA", [128, Tp], F32)
        self.fB = p.sbuf("fB", [128, Tp], F32)
        self.fC = p.sbuf("fC", [128, Tp], F32)
        self.fD = p.sbuf("fD", [128, Tp], F32)
        self.bA = p.sbuf("bA", [128, Tp], BF16)
        self.bB = p.sbuf("bB", [128, Tp], BF16)
        self.bC = p.sbuf("bC", [128, Tp], BF16)
        self.bD = p.sbuf("bD", [128, Tp], BF16)
        self.lrow = p.sbuf("lrow", [1, 3 * Tp], F32)
        self.r1slots = [p.sbuf(f"r1s{i}", [128, DC, 128], BF16) for i in range(4)]
        self.r2slots = [p.sbuf(f"r2s{i}", [128, DC, 128], BF16) for i in range(4)]
        self.ph0 = p.psum("ph0", [128, psw], F32)
        self.ph1 = p.psum("ph1", [128, psw], F32)
        self.po0 = p.psum("po0", [128, psw], F32)
        self.po1 = p.psum("po1", [128, psw], F32)
        self.gsc = p.sbuf("g_sc", [128, 16], F32)
        self.gm8 = p.sbuf("g_m8", [128, 8], F32)
        self.grp = p.sbuf("g_rp", [128, 16], F32)
        self.gsel = p.sbuf("g_sel", [128, 16], F32)
        self.gsum = p.sbuf("g_sum", [128, 1], F32)
        self.grcp = p.sbuf("g_rcp", [128, 1], F32)
        self.gpr = p.sbuf("g_pr", [128, 16], F32)
        self.gwT_sb = p.sbuf("gwT_sb", [128, DC, 16], BF16)
        self.gb_row = p.sbuf("gb_row", [1, 16], F32)
        self.gbrep = p.sbuf("gbrep", [128, 16], F32)


class FFNCore:
    def __init__(self, p, c, bufs):
        self.__dict__.update(bufs.__dict__)
        self.p, self.c = p, c
        self.Tp = bufs.Tp_alloc
        self.NT = self.Tp // 512
        self.ring1 = WeightRing(p, "r1", bufs.r1slots)
        self.ring2 = WeightRing(p, "r2", bufs.r2slots)
        self.s_in = [p.sem("s_in0"), p.sem("s_in1")]
        self.s_out = p.sem("s_out")

    def emit_gb(self, gb_ap):
        p, nc = self.p, self.p.nc
        s_m = p.sem("s_misc_pe")
        s_a = p.sem("s_misc_act")
        s_d = p.sem("s_misc_dma")
        s_d.inc(nc.sync.dma_start(out=self.gb_row[:, :], in_=gb_ap), 16)
        s_d.wait(nc.tensor)
        s_m.inc(nc.tensor.matmul(self.po0[:, 0:16], self.c["ones_row_f"][:, :],
                                 self.gb_row[:, :], start=True, stop=True), 1)
        s_m.wait(nc.scalar)
        s_a.inc(nc.scalar.copy(self.gbrep[:, :], self.po0[:, 0:16]), 1)
        s_a.wait(nc.vector)
        p.barrier()

    def emit_ln(self, src, tag):
        p, nc, c, Tp, NT = self.p, self.p.nc, self.c, self.Tp, self.NT
        t, a, v, sy = nc.tensor, nc.scalar, nc.vector, nc.sync
        s_a = p.sem("s_ln_act")
        s_v = p.sem("s_ln_dve")
        s_m = p.sem("s_ln_pe")
        p.barrier()
        bufs = [self.fA, self.fB]
        kind = src[0]
        act_done_marks = [0] * DC
        pe_done_marks = [0] * DC
        for if_c in range(DC):
            if kind == "dram":
                ap_fm, col0 = src[1], src[2]
                par = if_c % 2
                buf = bufs[par]
                if if_c >= 2:
                    s_a.wait(sy, act_done_marks[if_c - 2])
                self.s_in[par].inc(sy.dma_start(
                    out=buf[:, :],
                    in_=ap_fm[if_c * 128:(if_c + 1) * 128, col0:col0 + Tp]), 16)
                xt = buf[:, :]
                self.s_in[par].wait(a)
            else:
                xt = src[1][:, if_c, :]
            xb = self.bA if if_c % 2 == 0 else self.bB
            sq = self.bC if if_c % 2 == 0 else self.bD
            if if_c >= 2:
                s_m.wait(a, pe_done_marks[if_c - 2])
            s_a.inc(a.copy(xb[:, :], xt), 1)
            s_a.inc(a.square(sq[:, :], xt), 1)
            act_done_marks[if_c] = s_a.n
            s_a.wait(t)
            for n in range(NT):
                s_m.inc(t.matmul(self.po0[0:1, n * 512:(n + 1) * 512],
                                 c["ones_col_bf"][:, :],
                                 xb[:, n * 512:(n + 1) * 512],
                                 start=(if_c == 0), stop=(if_c == DC - 1)), 1)
                s_m.inc(t.matmul(self.po1[0:1, n * 512:(n + 1) * 512],
                                 c["ones_col_bf"][:, :],
                                 sq[:, n * 512:(n + 1) * 512],
                                 start=(if_c == 0), stop=(if_c == DC - 1)), 1)
            pe_done_marks[if_c] = s_m.n
        M = self.lrow[:, 0:Tp]
        R = self.lrow[:, Tp:2 * Tp]
        X = self.lrow[:, 2 * Tp:3 * Tp]
        s_m.wait(a)
        s_a.inc(a.mul(M, self.po0[0:1, 0:Tp], 1.0 / D), 1)
        s_a.inc(a.mul(X, self.po1[0:1, 0:Tp], 1.0 / D), 1)
        s_a.wait(v)
        s_v.inc(v.tensor_mul(R, M, M), 1)
        v.drain()
        s_v.inc(v.tensor_sub(X, X, R), 1)
        v.drain()
        s_v.wait(a)
        s_a.inc(a.activation(out=R, in_=X, func=AF.Sqrt,
                             bias=self.c["eps"][0:1, 0:1]), 1)
        s_a.wait(v)
        s_v.inc(v.reciprocal(X, R), 1)
        v.drain()
        s_v.wait(t)
        for row, ps, dst in ((M, self.po0, self.fC), (X, self.po1, self.fD)):
            for n in range(NT):
                s_m.inc(t.matmul(ps[:, n * 512:(n + 1) * 512],
                                 c["ones_row_f"][:, :],
                                 row[:, n * 512:(n + 1) * 512],
                                 start=True, stop=True), 1)
            s_m.wait(a)
            s_a.inc(a.copy(dst[:, :], ps[:, 0:Tp]), 1)
        s_a.wait(v)
        p.barrier()
        dve_done_marks = [0] * DC
        for if_c in range(DC):
            if kind == "dram":
                ap_fm, col0 = src[1], src[2]
                par = if_c % 2
                buf = bufs[par]
                if if_c >= 2:
                    s_v.wait(sy, dve_done_marks[if_c - 2])
                self.s_in[par].inc(sy.dma_start(
                    out=buf[:, :],
                    in_=ap_fm[if_c * 128:(if_c + 1) * 128, col0:col0 + Tp]), 16)
                xt = buf[:, :]
                self.s_in[par].wait(v)
            else:
                xt = src[1][:, if_c, :]
            s_v.inc(v.tensor_sub(xt, xt, self.fC[:, :]), 1)
            v.drain()
            s_v.inc(v.tensor_mul(self.xln[:, if_c, :], xt, self.fD[:, :]), 1)
            v.drain()
            dve_done_marks[if_c] = s_v.n
        p.barrier()

    def emit_gate(self, gwT_ap):
        p, nc, c, Tp = self.p, self.p.nc, self.c, self.Tp
        t, a, v, g = nc.tensor, nc.scalar, nc.vector, nc.gpsimd
        s_d = p.sem("s_g_dma")
        s_m = p.sem("s_g_pe")
        s_v = p.sem("s_g_dve")
        s_a = p.sem("s_g_act")
        s_d.inc(g.dma_start(out=self.gwT_sb[:, :, :], in_=gwT_ap), 16)
        s_d.wait(t)
        sig_mark = 0
        cp_mark = 0
        for ti in range(Tp // 128):
            if sig_mark:
                s_a.wait(t, sig_mark)
            for if_c in range(DC):
                s_m.inc(t.matmul(self.ph0[:, 0:16],
                                 self.xln[:, if_c, ti * 128:(ti + 1) * 128],
                                 self.gwT_sb[:, if_c, :],
                                 start=(if_c == 0), stop=(if_c == DC - 1)), 1)
            s_m.wait(a)
            s_a.inc(a.activation(out=self.gsc[:, :], in_=self.ph0[:, 0:16],
                                 func=AF.Sigmoid), 1)
            sig_mark = s_a.n
            s_a.wait(v)
            s_v.inc(v.tensor_add(self.gsc[:, :], self.gsc[:, :], self.gbrep[:, :]), 1)
            v.drain()
            v.max(out=self.gm8[:, :], in_=self.gsc[:, :])
            v.drain()
            v.memset(self.gm8[:, 4:8], NEG_BIG)
            v.drain()
            v.match_replace(out=self.grp[:, :], in_to_replace=self.gm8[:, :],
                            in_values=self.gsc[:, :], imm_value=NEG_BIG)
            v.drain()
            v.tensor_tensor(out=self.gsel[:, :], in0=self.gsc[:, :],
                            in1=self.grp[:, :], op=ALU.not_equal)
            v.drain()
            v.tensor_mul(self.gsel[:, :], self.gsel[:, :], self.gsc[:, :])
            v.drain()
            v.reduce_sum(self.gsum[:, :], self.gsel[:, :], axis=AX.X)
            v.drain()
            v.reciprocal(self.grcp[:, :], self.gsum[:, :])
            v.drain()
            s_v.inc(v.tensor_scalar(out=self.gpr[:, :], in0=self.gsel[:, :],
                                    scalar1=self.grcp[:, 0:1], scalar2=None,
                                    op0=ALU.mult), 1)
            v.drain()
            s_v.wait(t)
            if cp_mark:
                s_a.wait(t, cp_mark)
            s_m.inc(t.transpose(self.ph1[0:16, 0:128], self.gpr[:, :],
                                c["ident_f"][:, :]), 1)
            s_m.wait(a)
            s_a.inc(a.copy(self.probs_fm[:, ti * 128:(ti + 1) * 128],
                           self.ph1[0:16, 0:128]), 1)
            cp_mark = s_a.n
        p.barrier()

    def emit_experts(self, w1r, w3r, w2r, w1s, w3s, w2s):
        p, nc, c, Tp, NT = self.p, self.p.nc, self.c, self.Tp, self.NT
        t, a, v = nc.tensor, nc.scalar, nc.vector
        s_v = p.sem("s_e_dve")
        s_a = p.sem("s_e_act")
        s_m1 = p.sem("s_e_mm1")
        s_m2 = p.sem("s_e_mm2")
        s_pr = p.sem("s_e_prep")
        s_prm = p.sem("s_e_prepm")
        s_h3 = p.sem("s_e_h3")

        v.memset(self.acc[:, :, :], 0.0)
        v.drain()
        p.barrier()

        experts = list(range(len(w1r))) + (["shared"] if w1s is not None else [])
        for e in experts:
            W1, W3, W2 = (w1s, w3s, w2s) if e == "shared" else (w1r[e], w3r[e], w2r[e])
            for hf_c in range(HC):
                self.ring1.enqueue(W1[hf_c])
                self.ring1.enqueue(W3[hf_c])
            for of_c in range(DC):
                self.ring2.enqueue(W2[of_c])

        h3_marks = [0] * HC
        po_evac_mark = [0, 0]
        ph_free_a = 0
        ph_free_v = 0
        prep_mark = 0

        for ei, e in enumerate(experts):
            routed = e != "shared"
            if routed:
                if po_evac_mark[0]:
                    s_v.wait(t, po_evac_mark[0])
                for n in range(NT):
                    s_prm.inc(t.matmul(self.po0[:, n * 512:(n + 1) * 512],
                                       c["sel"][:, e * 128:(e + 1) * 128],
                                       self.probs_fm[:, n * 512:(n + 1) * 512],
                                       start=True, stop=True), 1)
                s_prm.wait(a)
                s_pr.inc(a.copy(self.probs_rep[:, :], self.po0[:, 0:Tp]), 1)
                prep_mark = s_pr.n
            for hf_c in range(HC):
                if ph_free_a:
                    s_a.wait(t, ph_free_a)
                if ph_free_v:
                    s_v.wait(t, ph_free_v)
                w1sl = self.ring1.use()
                for if_c in range(DC):
                    for n in range(NT):
                        mm = t.matmul(self.ph0[:, n * 512:(n + 1) * 512],
                                      w1sl[:, if_c, :],
                                      self.xln[:, if_c, n * 512:(n + 1) * 512],
                                      start=(if_c == 0), stop=(if_c == DC - 1))
                s_m1.inc(mm, 1)
                self.ring1.mark(s_m1, s_m1.n)
                w3sl = self.ring1.use()
                for if_c in range(DC):
                    for n in range(NT):
                        mm = t.matmul(self.ph1[:, n * 512:(n + 1) * 512],
                                      w3sl[:, if_c, :],
                                      self.xln[:, if_c, n * 512:(n + 1) * 512],
                                      start=(if_c == 0), stop=(if_c == DC - 1))
                s_m1.inc(mm, 1)
                self.ring1.mark(s_m1, s_m1.n)
                par = hf_c % 2
                fa = self.fA if par == 0 else self.fB
                fb = self.fC if par == 0 else self.fD
                sb = self.bA if par == 0 else self.bB
                s_m1.wait(a)
                s_m1.wait(v)
                s_a.inc(a.copy(fa[:, :], self.ph0[:, 0:Tp]), 1)
                ph_free_a = s_a.n
                s_v.inc(v.tensor_copy(fb[:, :], self.ph1[:, 0:Tp]), 1)
                v.drain()
                ph_free_v = s_v.n
                if routed:
                    s_a.wait(v)
                    s_v.inc(v.tensor_mul(fb[:, :], fa[:, :], fb[:, :]), 1)
                    v.drain()
                    s_v.wait(a)
                    s_a.inc(a.activation(out=sb[:, :], in_=fb[:, :], func=AF.Silu), 1)
                    s_a.wait(v)
                    s_pr.wait(v, prep_mark)
                    s_h3.inc(v.tensor_mul(self.h3[:, hf_c, :], sb[:, :],
                                          self.probs_rep[:, :]), 1)
                    v.drain()
                else:
                    a.drain()
                    s_a.inc(a.activation(out=sb[:, :], in_=fa[:, :], func=AF.Silu), 1)
                    s_a.wait(v)
                    s_h3.inc(v.tensor_mul(self.h3[:, hf_c, :], sb[:, :], fb[:, :]), 1)
                    v.drain()
                h3_marks[hf_c] = s_h3.n
            for of_c in range(DC):
                par = of_c % 2
                ps = self.po0 if par == 0 else self.po1
                if par == 0 and routed:
                    s_pr.wait(t, prep_mark)
                if po_evac_mark[par]:
                    s_v.wait(t, po_evac_mark[par])
                w2sl = self.ring2.use()
                for hf_c in range(HC):
                    s_h3.wait(t, h3_marks[hf_c])
                    for n in range(NT):
                        mm = t.matmul(ps[:, n * 512:(n + 1) * 512],
                                      w2sl[:, hf_c, :],
                                      self.h3[:, hf_c, n * 512:(n + 1) * 512],
                                      start=(hf_c == 0), stop=(hf_c == HC - 1))
                s_m2.inc(mm, 1)
                self.ring2.mark(s_m2, s_m2.n)
                s_m2.wait(v)
                s_v.inc(v.tensor_add(self.acc[:, of_c, :], self.acc[:, of_c, :],
                                     ps[:, 0:Tp]), 1)
                v.drain()
                po_evac_mark[par] = s_v.n
        p.barrier()

    def emit_resid(self, x_fm_ap, col0, scale, out_fm_ap=None, out_col=None):
        p, nc = self.p, self.p.nc
        v, sy = nc.vector, nc.sync
        s_v = p.sem("s_r_dve")
        bufs = [self.fA, self.fB]
        marks = [0] * DC
        p.barrier()
        if out_col is None:
            out_col = col0
        for of_c in range(DC):
            par = of_c % 2
            buf = bufs[par]
            if of_c >= 2:
                s_v.wait(sy, marks[of_c - 2])
            self.s_in[par].inc(sy.dma_start(
                out=buf[:, :],
                in_=x_fm_ap[of_c * 128:(of_c + 1) * 128, col0:col0 + self.Tp]), 16)
            self.s_in[par].wait(v)
            s_v.inc(v.scalar_tensor_tensor(
                out=self.acc[:, of_c, :], in0=self.acc[:, of_c, :],
                scalar=scale, in1=buf[:, :], op0=ALU.mult, op1=ALU.add), 1)
            v.drain()
            marks[of_c] = s_v.n
            if out_fm_ap is not None:
                s_v.wait(sy, s_v.n)
                if isinstance(out_col, tuple):
                    _, scale, cstat, mdma_fn = out_col
                    _of = of_c

                    def _emit(mm, _of=_of, cstat=cstat, scale=scale):
                        c0 = mm * scale + cstat
                        return sy.dma_start(
                            out=out_fm_ap[_of * 128:(_of + 1) * 128,
                                          c0:c0 + self.Tp],
                            in_=self.acc[:, _of, :])

                    mdma_fn(sy, self.s_out, _emit)
                else:
                    osl = out_fm_ap[of_c * 128:(of_c + 1) * 128,
                                    out_col:out_col + self.Tp]
                    self.s_out.inc(sy.dma_start(out=osl,
                                                in_=self.acc[:, of_c, :]), 16)
        self.s_out.wait(nc.sync)
        p.barrier()

    def emit_proj(self, wT_ap, n_out_chunks, sink, tag, rhs=None):
        p, nc = self.p, self.p.nc
        t = nc.tensor
        s_m = p.sem(f"s_{tag}_pe")
        rhs = rhs if rhs is not None else self.xln
        NT = self.Tp // 512
        for oc in range(n_out_chunks):
            self.ring2.enqueue(wT_ap[oc])
        evac = {0: None, 1: None}
        for oc in range(n_out_chunks):
            par = oc % 2
            ps = self.po0 if par == 0 else self.po1
            if evac[par] is not None:
                sem, val = evac[par]
                sem.wait(t, val)
            sl = self.ring2.use()
            for if_c in range(DC):
                for n in range(NT):
                    mm = t.matmul(ps[:, n * 512:(n + 1) * 512],
                                  sl[:, if_c, :],
                                  rhs[:, if_c, n * 512:(n + 1) * 512],
                                  start=(if_c == 0), stop=(if_c == DC - 1))
            s_m.inc(mm, 1)
            self.ring2.mark(s_m, s_m.n)
            evac[par] = sink(oc, ps, s_m, s_m.n)
        p.barrier()


def build_l1(Ttok, Tp):
    n_pass = Ttok // Tp
    nc = bass.Bass("TRN2", target_bir_lowering=False)
    ctx = ExitStack()
    p = Prog(nc, ctx)
    x_d = nc.dram_tensor("x", [D, Ttok], F32, kind="ExternalInput").ap()
    w1s = nc.dram_tensor("w1s", [HC, 128, DC, 128], F32, kind="ExternalInput").ap()
    w3s = nc.dram_tensor("w3s", [HC, 128, DC, 128], F32, kind="ExternalInput").ap()
    w2s = nc.dram_tensor("w2s", [DC, 128, HC, 128], F32, kind="ExternalInput").ap()
    w1r = nc.dram_tensor("w1r", [NE, HC, 128, DC, 128], F32, kind="ExternalInput").ap()
    w3r = nc.dram_tensor("w3r", [NE, HC, 128, DC, 128], F32, kind="ExternalInput").ap()
    w2r = nc.dram_tensor("w2r", [NE, DC, 128, HC, 128], F32, kind="ExternalInput").ap()
    gwT = nc.dram_tensor("gwT", [128, DC, 16], F32, kind="ExternalInput").ap()
    gb = nc.dram_tensor("gb", [1, 16], F32, kind="ExternalInput").ap()
    wqkvT = nc.dram_tensor("wqkvT", [3 * DC, 128, DC, 128], F32,
                           kind="ExternalInput").ap()
    y1_d = nc.dram_tensor("y1", [D, Ttok], F32, kind="ExternalOutput").ap()
    qkv_d = nc.dram_tensor("qkv", [3 * D, Ttok], BF16, kind="ExternalOutput").ap()

    with ctx:
        c = emit_consts(p)
        f = FFNCore(p, c, FFNBufs(p, Tp))
        f.emit_gb(gb)
        sy = nc.sync
        bq = [f.bC, f.bD]
        s_qa = p.sem("s_qkv_act")
        s_qo = [p.sem("s_qo0"), p.sem("s_qo1")]
        for ps_i in range(n_pass):
            col0 = ps_i * Tp
            f.emit_ln(("dram", x_d, col0), f"ln1_{ps_i}")
            f.emit_gate(gwT)
            f.emit_experts([w1r[e] for e in range(NE)], [w3r[e] for e in range(NE)],
                           [w2r[e] for e in range(NE)], w1s, w3s, w2s)
            f.emit_resid(x_d, col0, 0.5, out_fm_ap=y1_d)
            f.emit_ln(("sbuf", f.acc), f"ln2_{ps_i}")

            def sink(oc, psum_tile, pe_sem, pe_val, _col0=col0):
                a = nc.scalar
                par = oc % 2
                pe_sem.wait(a, pe_val)
                buf = bq[par]
                s_qo[par].wait(a)
                s_qa.inc(a.copy(buf[:, :], psum_tile[:, 0:Tp]), 1)
                s_qa.wait(sy, s_qa.n)
                s_qo[par].inc(sy.dma_start(
                    out=qkv_d[oc * 128:(oc + 1) * 128, _col0:_col0 + Tp],
                    in_=buf[:, :]), 16)
                return (s_qa, s_qa.n)

            f.emit_proj(wqkvT, 3 * DC, sink, f"qkv_{ps_i}")
            s_qo[0].wait(sy)
            s_qo[1].wait(sy)
        f.s_out.wait(sy)
        s_qo[0].wait(sy)
        s_qo[1].wait(sy)
        p.barrier()
    return nc


def build_l2(S):
    NP = 4
    SC = S // 1024
    nc = bass.Bass("TRN2", target_bir_lowering=False)
    ctx = ExitStack()
    p = Prog(nc, ctx)
    q_d = nc.dram_tensor("q", [NP, 128, S], BF16, kind="ExternalInput").ap()
    k_d = nc.dram_tensor("k", [NP, 128, S], BF16, kind="ExternalInput").ap()
    vt_d = nc.dram_tensor("vt", [NP, 128, S // 128, 128], BF16,
                          kind="ExternalInput").ap()
    o_d = nc.dram_tensor("o", [NP, 128, S], BF16, kind="ExternalOutput").ap()

    with ctx:
        c = emit_consts(p, need_sel=False, need_tri=True)
        t, a, v, sy, g = nc.tensor, nc.scalar, nc.vector, nc.sync, nc.gpsimd
        Ksb = p.sbuf("Ksb", [128, S], BF16)
        Qsb = p.sbuf("Qsb", [128, S], BF16)
        Vsb = p.sbuf("Vsb", [128, S // 128, 128], BF16)
        Pb = [p.sbuf(f"Pb{i}", [128, 1024], BF16) for i in range(2)]
        sqb = p.sbuf("sqb", [128, 1024], BF16)
        nrow = p.sbuf("nrow", [1, 2 * S], F32)
        srow = p.sbuf("srow", [1, 1024], F32)
        negc = p.sbuf("negc", [128, 1], F32)
        nsc = p.sbuf("nsc", [1, 4], F32)
        rrep = p.sbuf("rrep", [128, 1024], F32)
        oev = p.sbuf("oev", [128, 1024], BF16)
        psS = [p.psum(f"psS{i}", [128, 1024], F32) for i in range(2)]
        psO = p.psum("psO", [128, 1024], F32)
        psL = p.psum("psL", [128, 1024], F32)
        s_d = p.sem("s_dma")
        s_m = p.sem("s_pe")
        s_a = p.sem("s_act")
        s_v = p.sem("s_dve")
        s_o = p.sem("s_out")
        oev_mark = 0
        exp_mark = {0: 0, 1: 0}

        for pr in range(NP):
            p.barrier()
            s_d.inc(sy.dma_start(out=Ksb[:, :], in_=k_d[pr, :, :]), 16)
            s_d.inc(sy.dma_start(out=Qsb[:, :], in_=q_d[pr, :, :]), 16)
            s_d.inc(sy.dma_start(out=Vsb[:, :, :], in_=vt_d[pr, :, :, :]), 16)
            s_d.wait(a)
            ncp_mark = 0
            for which, src, off in (("k", Ksb, 0), ("q", Qsb, S)):
                for sc_i in range(S // 1024):
                    s_a.inc(a.square(sqb[:, :],
                                     src[:, sc_i * 1024:(sc_i + 1) * 1024]), 1)
                    s_a.wait(t)
                    if ncp_mark:
                        s_a.wait(t, ncp_mark)
                    for n in range(2):
                        s_m.inc(t.matmul(
                            psL[0:1, n * 512:(n + 1) * 512], c["ones_col_bf"][:, :],
                            sqb[:, n * 512:(n + 1) * 512], start=True, stop=True), 1)
                    s_m.wait(a)
                    s_a.inc(a.copy(nrow[:, off + sc_i * 1024: off + (sc_i + 1) * 1024],
                                   psL[0:1, 0:1024]), 1)
                    ncp_mark = s_a.n
            s_a.wait(v)
            v.reduce_max(nsc[:, 0:1], nrow[:, 0:S], axis=AX.X)
            v.drain()
            v.reduce_max(nsc[:, 1:2], nrow[:, S:2 * S], axis=AX.X)
            v.drain()
            s_v.inc(v.tensor_mul(nsc[:, 2:3], nsc[:, 0:1], nsc[:, 1:2]), 1)
            v.drain()
            s_v.wait(a)
            s_a.inc(a.activation(out=nsc[:, 3:4], in_=nsc[:, 2:3], func=AF.Sqrt), 1)
            s_a.wait(v)
            s_v.inc(v.tensor_scalar(out=nsc[:, 3:4], in0=nsc[:, 3:4],
                                    scalar1=-ISCALE, scalar2=None, op0=ALU.mult), 1)
            v.drain()
            s_v.wait(t)
            s_m.inc(t.matmul(psS[0][:, 0:1], c["ones_row_f"][:, :], nsc[:, 3:4],
                             start=True, stop=True), 1)
            s_m.wait(a)
            s_a.inc(a.copy(negc[:, :], psS[0][:, 0:1]), 1)
            a.drain()
            exp_mark = {0: s_a.n, 1: exp_mark[1]}
            for jq in range(SC):
                q0 = jq * 1024
                nkt = 8 * jq + 8
                for kt in range(nkt):
                    par = kt % 2
                    S_ps = psS[par]
                    diag = kt * 128 >= q0
                    nof = kt * 128 - q0 if diag else 0
                    if exp_mark[par]:
                        s_a.wait(t, exp_mark[par])
                    segs = []
                    x0 = nof
                    while x0 < 1024:
                        w = min(512 - (x0 % 512), 1024 - x0)
                        segs.append((x0, w))
                        x0 += w
                    for x0, w in segs:
                        s_m.inc(t.matmul(
                            S_ps[:, x0:x0 + w],
                            Ksb[:, kt * 128:(kt + 1) * 128],
                            Qsb[:, q0 + x0:q0 + x0 + w],
                            start=True, stop=True), 1)
                    s_m.wait(a)
                    s_a.inc(a.activation(out=Pb[par][:, nof:1024],
                                         in_=S_ps[:, nof:1024], func=AF.Exp,
                                         bias=negc[:, 0:1], scale=ISCALE), 1)
                    exp_mark[par] = s_a.n
                    if diag:
                        s_a.wait(v, s_a.n)
                        if nof > 0:
                            v.memset(Pb[par][:, 0:nof], 0.0)
                            v.drain()
                        s_v.inc(v.tensor_mul(Pb[par][:, nof:nof + 128],
                                             Pb[par][:, nof:nof + 128],
                                             c["tri"][:, :]), 1)
                        v.drain()
                        s_v.wait(t, s_v.n)
                    else:
                        s_a.wait(t, s_a.n)
                    for n in range(2):
                        s_m.inc(t.matmul(
                            psO[:, n * 512:(n + 1) * 512],
                            Vsb[:, kt, :],
                            Pb[par][:, n * 512:(n + 1) * 512],
                            start=(kt == 0), stop=(kt == nkt - 1)), 1)
                        s_m.inc(t.matmul(
                            psL[0:1, n * 512:(n + 1) * 512],
                            c["ones_col_bf"][:, :],
                            Pb[par][:, n * 512:(n + 1) * 512],
                            start=(kt == 0), stop=(kt == nkt - 1)), 1)
                s_m.wait(v)
                s_v.inc(v.reciprocal(srow[:, :], psL[0:1, 0:1024]), 1)
                v.drain()
                s_v.wait(t)
                for n in range(2):
                    s_m.inc(t.matmul(psS[0][:, n * 512:(n + 1) * 512],
                                     c["ones_row_f"][:, :],
                                     srow[:, n * 512:(n + 1) * 512],
                                     start=True, stop=True), 1)
                s_m.wait(a)
                s_a.inc(a.copy(rrep[:, :], psS[0][:, :]), 1)
                exp_mark[0] = s_a.n
                s_a.wait(v)
                if oev_mark:
                    s_o.wait(v, oev_mark)
                s_v.inc(v.tensor_mul(oev[:, :], psO[:, :], rrep[:, :]), 1)
                v.drain()
                s_v.wait(sy)
                s_o.inc(sy.dma_start(out=o_d[pr, :, q0:q0 + 1024], in_=oev[:, :]), 16)
                oev_mark = s_o.n
                s_v.wait(t)
        s_o.wait(sy)
        p.barrier()
    return nc


def build_l3(Ttok, Tp):
    n_pass = Ttok // Tp
    nc = bass.Bass("TRN2", target_bir_lowering=False)
    ctx = ExitStack()
    p = Prog(nc, ctx)
    at_d = nc.dram_tensor("attn", [128, DC, Ttok], BF16, kind="ExternalInput").ap()
    y1_d = nc.dram_tensor("y1", [D, Ttok], F32, kind="ExternalInput").ap()
    woT = nc.dram_tensor("woT", [DC, 128, DC, 128], F32, kind="ExternalInput").ap()
    w1s = nc.dram_tensor("w1s", [HC, 128, DC, 128], F32, kind="ExternalInput").ap()
    w3s = nc.dram_tensor("w3s", [HC, 128, DC, 128], F32, kind="ExternalInput").ap()
    w2s = nc.dram_tensor("w2s", [DC, 128, HC, 128], F32, kind="ExternalInput").ap()
    w1r = nc.dram_tensor("w1r", [NE, HC, 128, DC, 128], F32, kind="ExternalInput").ap()
    w3r = nc.dram_tensor("w3r", [NE, HC, 128, DC, 128], F32, kind="ExternalInput").ap()
    w2r = nc.dram_tensor("w2r", [NE, DC, 128, HC, 128], F32, kind="ExternalInput").ap()
    gwT = nc.dram_tensor("gwT", [128, DC, 16], F32, kind="ExternalInput").ap()
    gb = nc.dram_tensor("gb", [1, 16], F32, kind="ExternalInput").ap()
    y3_d = nc.dram_tensor("y3", [D, Ttok], F32, kind="ExternalOutput").ap()
    y2_s = nc.dram_tensor("y2s", [D, Ttok], F32, kind="Internal").ap()

    with ctx:
        c = emit_consts(p)
        f = FFNCore(p, c, FFNBufs(p, Tp))
        f.emit_gb(gb)
        sy, a, v = nc.sync, nc.scalar, nc.vector
        attn_sb = p.sbuf("attn_sb", [128, DC, Tp], BF16)
        s_ad = p.sem("s_attn_dma")
        s_y2 = p.sem("s_y2")
        s_y1i = [p.sem("s_y1i0"), p.sem("s_y1i1")]
        s_y2o = [p.sem("s_y2o0"), p.sem("s_y2o1")]
        for ps_i in range(n_pass):
            col0 = ps_i * Tp
            p.barrier()
            s_ad.inc(sy.dma_start(
                out=attn_sb[:, :, :],
                in_=at_d[:, :, col0:col0 + Tp]), 16)
            s_ad.wait(nc.tensor)

            def sink(oc, psum_tile, pe_sem, pe_val, _col0=col0):
                par = oc % 2
                buf = f.fC if par == 0 else f.fD
                s_y2o[par].wait(sy)
                s_y1i[par].inc(sy.dma_start(
                    out=buf[:, :],
                    in_=y1_d[oc * 128:(oc + 1) * 128, _col0:_col0 + Tp]), 16)
                s_y1i[par].wait(v)
                pe_sem.wait(v, pe_val)
                s_y2.inc(v.tensor_add(buf[:, :], buf[:, :], psum_tile[:, 0:Tp]), 1)
                v.drain()
                s_y2.wait(sy, s_y2.n)
                s_y2o[par].inc(sy.dma_start(
                    out=y2_s[oc * 128:(oc + 1) * 128, _col0:_col0 + Tp],
                    in_=buf[:, :]), 16)
                return (s_y2, s_y2.n)

            f.emit_proj(woT, DC, sink, f"wo_{ps_i}", rhs=attn_sb)
            s_y2o[0].wait(sy)
            s_y2o[1].wait(sy)
            p.barrier()
            f.emit_ln(("dram", y2_s, col0), f"ln3_{ps_i}")
            f.emit_gate(gwT)
            f.emit_experts([w1r[e] for e in range(NE)], [w3r[e] for e in range(NE)],
                           [w2r[e] for e in range(NE)], w1s, w3s, w2s)
            f.emit_resid(y2_s, col0, 0.5, out_fm_ap=y3_d)
        f.s_out.wait(sy)
        p.barrier()
    return nc


_cache = {}


def _programs():
    if "l1" not in _cache:
        _cache["l1"] = build_l1(TTOK, TP)
        _cache["l2"] = build_l2(SEQ)
        _cache["l3"] = build_l3(TTOK, 512)
    return _cache["l1"], _cache["l2"], _cache["l3"]


def kernel(L0_attn_hidden_states, L0_attn_w_qkv, L0_attn_w_o,
           L0_ffn_gate_w, L0_ffn_gate_b, L0_ffn_w1_shared, L0_ffn_w2_shared,
           L0_ffn_w3_shared, L0_ffn_w1_routed, L0_ffn_w2_routed,
           L0_ffn_w3_routed):
    x = np.ascontiguousarray(np.asarray(L0_attn_hidden_states, dtype=np.float32))
    flat = x.reshape(B * SEQ, D)
    w1s = host_pieces(np.asarray(L0_ffn_w1_shared, np.float32), DC, HC)
    w3s = host_pieces(np.asarray(L0_ffn_w3_shared, np.float32), DC, HC)
    w2s = host_pieces(np.asarray(L0_ffn_w2_shared, np.float32), HC, DC)
    w1r = np.stack([host_pieces(np.asarray(L0_ffn_w1_routed[e], np.float32),
                                DC, HC) for e in range(NE)])
    w3r = np.stack([host_pieces(np.asarray(L0_ffn_w3_routed[e], np.float32),
                                DC, HC) for e in range(NE)])
    w2r = np.stack([host_pieces(np.asarray(L0_ffn_w2_routed[e], np.float32),
                                HC, DC) for e in range(NE)])
    gwT = np.ascontiguousarray(np.asarray(L0_ffn_gate_w, np.float32).T
                               .reshape(DC, 128, 16).transpose(1, 0, 2))
    gb = np.ascontiguousarray(np.asarray(L0_ffn_gate_b, np.float32)[None, :])
    wqkvT = host_pieces(np.ascontiguousarray(
        np.asarray(L0_attn_w_qkv, np.float32).T), DC, 3 * DC)
    woT = host_pieces(np.ascontiguousarray(
        np.asarray(L0_attn_w_o, np.float32).T), DC, DC)

    l1nc, l2nc, l3nc = _programs()
    base = {"w1s": w1s, "w3s": w3s, "w2s": w2s, "w1r": w1r, "w3r": w3r,
            "w2r": w2r, "gwT": gwT, "gb": gb}

    l1_ins = [dict(base, wqkvT=wqkvT,
                   x=np.ascontiguousarray(flat[c * TTOK:(c + 1) * TTOK].T))
              for c in range(NCORES)]
    r1 = run_bass_kernel_spmd(l1nc, l1_ins, core_ids=list(range(NCORES)))
    y1 = [np.asarray(r1.results[c]["y1"]) for c in range(NCORES)]
    qkv = [np.asarray(r1.results[c]["qkv"]) for c in range(NCORES)]

    def headrows(kind, h):
        off = {"q": 0, "k": D, "v": 2 * D}[kind]
        return slice(off + h * 128, off + (h + 1) * 128)

    l2_ins = []
    for c2 in range(NCORES):
        b = c2 // 4
        hs = [4 * (c2 % 4) + i for i in range(4)]
        qs, ks, vts = [], [], []
        for h in hs:
            qfm = np.concatenate([qkv[4 * b + j][headrows("q", h)]
                                  for j in range(4)], axis=1)
            kfm = np.concatenate([qkv[4 * b + j][headrows("k", h)]
                                  for j in range(4)], axis=1)
            vfm = np.concatenate([qkv[4 * b + j][headrows("v", h)]
                                  for j in range(4)], axis=1)
            vt = np.ascontiguousarray(
                vfm.T.reshape(SEQ // 128, 128, 128).transpose(1, 0, 2))
            qs.append(qfm)
            ks.append(kfm)
            vts.append(vt)
        l2_ins.append({"q": np.ascontiguousarray(np.stack(qs)),
                       "k": np.ascontiguousarray(np.stack(ks)),
                       "vt": np.stack(vts)})
    r2 = run_bass_kernel_spmd(l2nc, l2_ins, core_ids=list(range(NCORES)))

    l3_ins = []
    for c in range(NCORES):
        b = c // 4
        colsl = slice((c % 4) * TTOK, (c % 4 + 1) * TTOK)
        rows = []
        for h in range(16):
            c2 = 4 * b + h // 4
            pr = h % 4
            rows.append(np.asarray(r2.results[c2]["o"])[pr][:, colsl])
        attn = np.concatenate(rows, axis=0)
        attn = np.ascontiguousarray(
            attn.reshape(DC, 128, TTOK).transpose(1, 0, 2))
        l3_ins.append(dict(base, woT=woT, attn=attn, y1=y1[c]))
    r3 = run_bass_kernel_spmd(l3nc, l3_ins, core_ids=list(range(NCORES)))

    out = np.concatenate(
        [np.asarray(r3.results[c]["y3"]).T for c in range(NCORES)], axis=0)
    return np.ascontiguousarray(out.reshape(B, SEQ, D).astype(np.float32))


I32 = mybir.dt.int32


def build_fused(seq=SEQ, sa=5120, dbg=0, fdbg=4, fb=3):
    Tp = 512
    xt1 = sa // 2
    yt1 = seq // 2
    xt2 = sa // 2
    yt2 = (seq - sa) // 2
    assert xt1 % Tp == 0 and yt1 % Tp == 0 and yt2 % Tp == 0
    nc = bass.Bass("TRN2", target_bir_lowering=False)
    ctx = ExitStack()
    p = Prog(nc, ctx)
    rid = nc.dram_tensor("rid", [1, 4], I32, kind="ExternalInput").ap()
    x_d = nc.dram_tensor("x", [D, yt1], F32, kind="ExternalInput").ap()
    w1s = nc.dram_tensor("w1s", [HC, 128, DC, 128], F32, kind="ExternalInput").ap()
    w3s = nc.dram_tensor("w3s", [HC, 128, DC, 128], F32, kind="ExternalInput").ap()
    w2s = nc.dram_tensor("w2s", [DC, 128, HC, 128], F32, kind="ExternalInput").ap()
    w1r = nc.dram_tensor("w1r", [NE, HC, 128, DC, 128], F32, kind="ExternalInput").ap()
    w3r = nc.dram_tensor("w3r", [NE, HC, 128, DC, 128], F32, kind="ExternalInput").ap()
    w2r = nc.dram_tensor("w2r", [NE, DC, 128, HC, 128], F32, kind="ExternalInput").ap()
    gwT = nc.dram_tensor("gwT", [128, DC, 16], F32, kind="ExternalInput").ap()
    gb = nc.dram_tensor("gb", [1, 16], F32, kind="ExternalInput").ap()
    wqkvT = nc.dram_tensor("wqkvT", [3 * DC, 128, DC, 128], F32,
                           kind="ExternalInput").ap()
    woT = nc.dram_tensor("woT", [DC, 128, DC, 128], F32, kind="ExternalInput").ap()
    y3_d = nc.dram_tensor("y3", [D, xt2], F32, kind="ExternalOutput").ap()
    qsh = nc.dram_tensor("qsh", [D, seq], BF16, kind="Internal",
                         addr_space="Shared").ap()
    ksh = nc.dram_tensor("ksh", [D, seq], BF16, kind="Internal",
                         addr_space="Shared").ap()
    vsh = nc.dram_tensor("vsh", [D, seq], BF16, kind="Internal",
                         addr_space="Shared").ap()
    ash = nc.dram_tensor("ash", [128, DC, seq], BF16, kind="Internal",
                         addr_space="Shared").ap()
    y1sh = nc.dram_tensor("y1sh", [D, seq], F32, kind="Internal",
                          addr_space="Shared").ap()
    flg = nc.dram_tensor("flg", [2, 16], I32, kind="Internal",
                         addr_space="Shared").ap()
    y1loc = nc.dram_tensor("y1loc", [D, yt1], F32, kind="Internal").ap()
    qkvloc = nc.dram_tensor("qkvloc", [3 * D, yt1], BF16, kind="Internal").ap()
    y2loc = nc.dram_tensor("y2loc", [D, xt2], F32, kind="Internal").ap()
    y1tmp = nc.dram_tensor("y1tmp", [D, xt2], F32, kind="Internal").ap()
    ashloc = nc.dram_tensor("ashloc", [128, DC, xt2], BF16, kind="Internal").ap()

    with ctx:
        g, sy, t, a, v = nc.gpsimd, nc.sync, nc.tensor, nc.scalar, nc.vector
        c = emit_consts(p, need_sel=True, need_tri=True)
        c["ident_b"] = p.sbuf("ident_b", [128, 128], BF16)
        g.memset(c["ident_b"][:, :], 0.0)
        g.drain()
        g.affine_select(out=c["ident_b"][:, :], in_=c["ident_b"][:, :],
                        compare_op=ALU.not_equal, fill=1.0,
                        base=0, pattern=[[-1, 128]], channel_multiplier=1)
        g.drain()
        bufs = FFNBufs(p, Tp)
        Ksb = p.sbuf("Ksb", [128, seq], BF16)
        Qsb = p.sbuf("Qsb", [128, seq], BF16)
        Vsb = p.sbuf("Vsb", [128, seq // 128, 128], BF16)
        Vfm = p.sbuf("Vfm", [128, 1024], BF16)
        Pb = [p.sbuf(f"Pb{i}", [128, 512], BF16) for i in range(2)]
        sqb = p.sbuf("sqb", [128, 512], BF16)
        nrow = p.sbuf("nrow", [1, 2 * seq], F32)
        srow = p.sbuf("srow", [1, 512], F32)
        negc = p.sbuf("negc", [128, 1], F32)
        nsc = p.sbuf("nsc", [1, 4], F32)
        rrep = p.sbuf("rrep", [128, 512], F32)
        oev = p.sbuf("oev", [128, 512], BF16)
        attn_sb = p.sbuf("attn_sb", [128, DC, Tp], BF16)
        rb = p.sbuf("rb", [1, 4], I32)
        fw = p.sbuf("fw", [1, 16], I32)
        fr = p.sbuf("fr", [1, 16], I32)
        pvt = p.psum("pvt", [128, 512], BF16)
        s_id = p.sem("s_id")
        s_id.inc(sy.dma_start(out=rb[:, :], in_=rid), 16)
        engines = [sy, g, t, v, a]
        for e in engines:
            e.wait_ge(s_id.h, 16)
        regs = {}
        for e, nm in zip(engines, ["sy", "g", "t", "v", "a"]):
            r_role = e.alloc_register(f"role_{nm}")
            e.load(r_role, rb[0:1, 0:1])
            regs[nm] = r_role
        mem_g = g.alloc_register("mem_g")
        g.load(mem_g, rb[0:1, 1:2])
        mem_g_sv = g.snap(mem_g)
        pm_g = g.alloc_register("pm_g")
        g.reg_alu(pm_g, mem_g, 1, ALU.bitwise_xor)
        pm_g_sv = g.snap(pm_g)
        pollc = g.alloc_register("pollc")
        pollv = g.alloc_register("pollv")
        pollg = g.alloc_register("pollg")
        mem_sy = sy.alloc_register("mem_sy")
        sy.load(mem_sy, rb[0:1, 1:2])
        mem_sy_sv = sy.snap(mem_sy)
        mem_t = t.alloc_register("mem_t")
        t.load(mem_t, rb[0:1, 1:2])
        mem_t_sv = t.snap(mem_t)
        mem_a = a.alloc_register("mem_a")
        a.load(mem_a, rb[0:1, 1:2])
        mem_a_sv = a.snap(mem_a)
        mem_v = v.alloc_register("mem_v")
        v.load(mem_v, rb[0:1, 1:2])
        mem_v_sv = v.snap(mem_v)
        msv = {"sy": mem_sy_sv, "t": mem_t_sv, "a": mem_a_sv, "v": mem_v_sv}

        def flag_set(val):
            p.barrier()
            g.memset(fw[:, :], val)
            g.drain()
            s_fl = p.sem("s_flag")
            s_fl.inc(g.dma_start(out=flg[bass.ds(mem_g_sv, 1), :],
                                 in_=fw[:, :]), 16)
            g.wait_ge(s_fl.h, s_fl.n)

        def flag_wait(val):
            s_pl = p.sem(f"s_poll{val}")
            g.reg_mov(pollc, 16)
            g.reg_mov(pollg, 1)
            with g.While(pollg):
                g.dma_start(out=fr[:, :], in_=flg[bass.ds(pm_g_sv, 1), :]
                            ).then_inc(s_pl.h, 16)
                g.wait_ge(s_pl.h, pollc)
                g.reg_add(pollc, pollc, 16)
                g.load(pollv, fr[0:1, 0:1])
                g.reg_alu(pollg, pollv, val, ALU.is_lt)
            p.barrier()

        def emit_body(role):
            f = FFNCore(p, c, bufs)
            f.emit_gb(gb)
            n1 = (xt1 if role == 0 else yt1) // Tp
            n2 = (xt2 if role == 0 else yt2) // Tp
            t1tok = xt1 if role == 0 else yt1
            t2tok = xt2 if role == 0 else yt2
            t2base = 0 if role == 0 else sa
            bq = [f.bC, f.bD]
            s_qa = p.sem("s_qkv_act")
            s_qo = [p.sem("s_qo0"), p.sem("s_qo1")]
            for ps_i in range(n1):
                col = ps_i * Tp
                f.emit_ln(("dram", x_d, col), "ln1")
                f.emit_gate(gwT)
                f.emit_experts([w1r[e] for e in range(NE)],
                               [w3r[e] for e in range(NE)],
                               [w2r[e] for e in range(NE)], w1s, w3s, w2s)
                f.emit_resid(x_d, col, 0.5, out_fm_ap=y1loc)
                f.emit_ln(("sbuf", f.acc), "ln2")

                def qsink(oc, psum_tile, pe_sem, pe_val, _col=col):
                    par = oc % 2
                    pe_sem.wait(a, pe_val)
                    buf = bq[par]
                    s_qo[par].wait(a)
                    s_qa.inc(a.copy(buf[:, :], psum_tile[:, 0:Tp]), 1)
                    s_qa.wait(sy, s_qa.n)
                    s_qo[par].inc(sy.dma_start(
                        out=qkvloc[oc * 128:(oc + 1) * 128, _col:_col + Tp],
                        in_=buf[:, :]), 16)
                    return (s_qa, s_qa.n)

                f.emit_proj(wqkvT, 3 * DC, qsink, "qkv")
                s_qo[0].wait(sy)
                s_qo[1].wait(sy)
            f.s_out.wait(sy)
            s_sh = p.sem("s_share")
            for mm in nc.Switch(engines=[sy], index=[msv["sy"]], n=2):
                c0 = mm * t1tok
                for k3, dst in enumerate((qsh, ksh, vsh)):
                    s_sh.inc(sy.dma_start(
                        out=dst[:, c0:c0 + t1tok],
                        in_=qkvloc[k3 * D:(k3 + 1) * D, 0:t1tok]), 16, )
                s_sh.inc(sy.dma_start(
                    out=y1sh[:, c0:c0 + t1tok], in_=y1loc[:, 0:t1tok]), 16)
                s_sh.n -= 64
            s_sh.n += 64
            s_sh.wait(sy)
            flag_set(1)
            flag_wait(1)
            if dbg == 1:
                s_dbg = p.sem("s_dbg")
                s_dbg.inc(sy.dma_start(out=y3_d,
                                       in_=y1loc[:, 0:xt2]), 16)
                s_dbg.wait(sy)
                return
            psS = [f.ph0, f.ph1]
            psO = f.po0
            psL = f.po1
            nqc = t2tok // 512
            p.begin_arms()
            for mm in nc.Switch(engines=engines,
                                index=[msv["sy"], mem_g_sv, msv["t"],
                                       msv["v"], msv["a"]], n=2):
                s_d = p.sem("s_fl_dma")
                s_m = p.sem("s_fl_pe")
                s_a2 = p.sem("s_fl_act")
                s_v2 = p.sem("s_fl_dve")
                s_o2 = p.sem("s_fl_out")
                oev_mark = 0
                exp_mark = {0: 0, 1: 0}
                qbase = t2base + mm * t2tok
                KLm = qbase + t2tok
                for hl in range(16):
                    p.barrier()
                    r0 = hl * 128
                    s_d.inc(sy.dma_start(out=Ksb[:, 0:KLm],
                                         in_=ksh[r0:r0 + 128, 0:KLm]), 16)
                    s_d.inc(sy.dma_start(out=Qsb[:, 0:KLm],
                                         in_=qsh[r0:r0 + 128, 0:KLm]), 16)
                    vt_mark = 0
                    nch = ((KLm + 1023) // 1024) if fdbg >= 2 else 0
                    for chv in range(nch):
                        w1024 = min(1024, KLm - chv * 1024)
                        if vt_mark:
                            s_v2.wait(sy, vt_mark)
                        s_d.inc(sy.dma_start(
                            out=Vfm[:, 0:w1024],
                            in_=vsh[r0:r0 + 128,
                                    chv * 1024:chv * 1024 + w1024]), 16)
                        s_d.wait(t)
                        for j in range(w1024 // 128 if fb >= 2 else 0):
                            kt = chv * 8 + j
                            par4 = kt % 4
                            if kt >= 4:
                                s_v2.wait(t, s_v2.n - 3)
                            s_m.inc(t.transpose(
                                pvt[:, par4 * 128:(par4 + 1) * 128],
                                Vfm[:, j * 128:(j + 1) * 128],
                                c["ident_b"][:, :]), 1)
                            if fb >= 3:
                                s_m.wait(a)
                                s_v2.inc(a.copy(
                                    Vsb[:, kt, :],
                                    pvt[:, par4 * 128:(par4 + 1) * 128]), 1)
                                a.drain()
                        vt_mark = s_v2.n
                    if fdbg < 3:
                        continue
                    s_d.wait(a)
                    ncp = 0
                    for sc_i in range(KLm // 512):
                        for srcb, off in ((Ksb, 0), (Qsb, seq)):
                            s_a2.inc(a.square(sqb[:, 0:512],
                                              srcb[:, sc_i * 512:(sc_i + 1) * 512]), 1)
                            s_a2.wait(t)
                            if ncp:
                                s_a2.wait(t, ncp)
                            s_m.inc(t.matmul(
                                psL[0:1, 0:512], c["ones_col_bf"][:, :],
                                sqb[:, 0:512], start=True, stop=True), 1)
                            s_m.wait(a)
                            s_a2.inc(a.copy(
                                nrow[:, off + sc_i * 512: off + (sc_i + 1) * 512],
                                psL[0:1, 0:512]), 1)
                            ncp = s_a2.n
                    s_a2.wait(v)
                    v.reduce_max(nsc[:, 0:1], nrow[:, 0:KLm], axis=AX.X)
                    v.drain()
                    v.reduce_max(nsc[:, 1:2], nrow[:, seq:seq + KLm], axis=AX.X)
                    v.drain()
                    s_v2.inc(v.tensor_mul(nsc[:, 2:3], nsc[:, 0:1], nsc[:, 1:2]), 1)
                    v.drain()
                    s_v2.wait(a)
                    s_a2.inc(a.activation(out=nsc[:, 3:4], in_=nsc[:, 2:3],
                                          func=AF.Sqrt), 1)
                    s_a2.wait(v)
                    s_v2.inc(v.tensor_scalar(out=nsc[:, 3:4], in0=nsc[:, 3:4],
                                             scalar1=-ISCALE, scalar2=None,
                                             op0=ALU.mult), 1)
                    v.drain()
                    s_v2.wait(t)
                    s_m.inc(t.matmul(psS[0][:, 0:1], c["ones_row_f"][:, :],
                                     nsc[:, 3:4], start=True, stop=True), 1)
                    s_m.wait(a)
                    s_a2.inc(a.copy(negc[:, :], psS[0][:, 0:1]), 1)
                    a.drain()
                    exp_mark[0] = s_a2.n
                    for jcl in range(nqc if fdbg >= 4 else 0):
                        q0 = qbase + jcl * 512
                        nkt = q0 // 128 + 4
                        for kt in range(nkt):
                            par = kt % 2
                            S_ps = psS[par]
                            diag = kt * 128 >= q0
                            nof = kt * 128 - q0 if diag else 0
                            if exp_mark[par]:
                                s_a2.wait(t, exp_mark[par])
                            s_m.inc(t.matmul(
                                S_ps[:, nof:512],
                                Ksb[:, kt * 128:(kt + 1) * 128],
                                Qsb[:, q0 + nof:q0 + 512],
                                start=True, stop=True), 1)
                            s_m.wait(a)
                            s_a2.inc(a.activation(
                                out=Pb[par][:, nof:512], in_=S_ps[:, nof:512],
                                func=AF.Exp, bias=negc[:, 0:1], scale=ISCALE), 1)
                            exp_mark[par] = s_a2.n
                            if diag:
                                s_a2.wait(v, s_a2.n)
                                if nof > 0:
                                    v.memset(Pb[par][:, 0:nof], 0.0)
                                    v.drain()
                                s_v2.inc(v.tensor_mul(
                                    Pb[par][:, nof:nof + 128],
                                    Pb[par][:, nof:nof + 128],
                                    c["tri"][:, :]), 1)
                                v.drain()
                                s_v2.wait(t, s_v2.n)
                            else:
                                s_a2.wait(t, s_a2.n)
                            s_m.inc(t.matmul(psO[:, 0:512], Vsb[:, kt, :],
                                             Pb[par][:, 0:512],
                                             start=(kt == 0),
                                             stop=(kt == nkt - 1)), 1)
                            s_m.inc(t.matmul(psL[0:1, 0:512],
                                             c["ones_col_bf"][:, :],
                                             Pb[par][:, 0:512],
                                             start=(kt == 0),
                                             stop=(kt == nkt - 1)), 1)
                        s_m.wait(v)
                        s_v2.inc(v.reciprocal(srow[:, :], psL[0:1, 0:512]), 1)
                        v.drain()
                        s_v2.wait(t)
                        s_m.inc(t.matmul(psS[0][:, 0:512], c["ones_row_f"][:, :],
                                         srow[:, 0:512], start=True, stop=True), 1)
                        s_m.wait(a)
                        s_a2.inc(a.copy(rrep[:, :], psS[0][:, 0:512]), 1)
                        exp_mark[0] = s_a2.n
                        s_a2.wait(v)
                        if oev_mark:
                            s_o2.wait(v, oev_mark)
                        s_v2.inc(v.tensor_mul(oev[:, :], psO[:, 0:512],
                                              rrep[:, :]), 1)
                        v.drain()
                        s_v2.wait(sy)
                        s_o2.inc(sy.dma_start(
                            out=ash[:, hl, q0:q0 + 512], in_=oev[:, :]), 16)
                        oev_mark = s_o2.n
                        s_v2.wait(t)
                s_o2.wait(sy)
                p.end_arm()
            p.end_arms()
            flag_set(2)
            flag_wait(2)
            if dbg == 15:
                s_dbg = p.sem("s_dbg")
                s_dbg.inc(sy.dma_start(out=y3_d, in_=y1sh[:, 0:xt2]), 16)
                s_dbg.wait(sy)
                return
            s_y2 = p.sem("s_y2")
            s_y1i = [p.sem("s_y1i0"), p.sem("s_y1i1")]
            s_y2o = [p.sem("s_y2o0"), p.sem("s_y2o1")]
            s_ad = p.sem("s_attn_dma")
            s_ph3 = p.sem("s_ph3")
            for mm in nc.Switch(engines=[sy], index=[msv["sy"]], n=2):
                t0g = t2base + mm * t2tok
                sy.dma_start(
                    out=ashloc[:, :, 0:t2tok],
                    in_=ash[:, :, t0g:t0g + t2tok]).then_inc(s_ph3.h, 16)
                sy.dma_start(out=y1tmp[:, 0:t2tok],
                             in_=y1sh[:, t0g:t0g + t2tok]).then_inc(s_ph3.h, 16)
            s_ph3.n += 32
            s_ph3.wait(sy)
            if dbg == 2:
                s_dbg = p.sem("s_dbg")
                s_dbg.inc(sy.dma_start(out=y3_d,
                                       in_=y1tmp[:, 0:xt2]), 16)
                s_dbg.wait(sy)
                return
            for pf in range(n2):
                relc = pf * Tp
                p.barrier()
                s_ad.inc(sy.dma_start(
                    out=attn_sb[:, :, :],
                    in_=ashloc[:, :, relc:relc + Tp]), 16)
                s_ad.wait(t)

                def wsink(oc, psum_tile, pe_sem, pe_val, _relc=relc):
                    par = oc % 2
                    buf = f.fC if par == 0 else f.fD
                    s_y2o[par].wait(sy)
                    s_ad.wait(sy)
                    s_y1i[par].inc(sy.dma_start(
                        out=buf[:, :],
                        in_=y1tmp[oc * 128:(oc + 1) * 128, _relc:_relc + Tp]), 16)
                    s_y1i[par].wait(v)
                    pe_sem.wait(v, pe_val)
                    s_y2.inc(v.tensor_add(buf[:, :], buf[:, :],
                                          psum_tile[:, 0:Tp]), 1)
                    v.drain()
                    s_y2.wait(sy, s_y2.n)
                    s_y2o[par].inc(sy.dma_start(
                        out=y2loc[oc * 128:(oc + 1) * 128, _relc:_relc + Tp],
                        in_=buf[:, :]), 16)
                    return (s_y2, s_y2.n)

                f.emit_proj(woT, DC, wsink, "wo", rhs=attn_sb)
                s_y2o[0].wait(sy)
                s_y2o[1].wait(sy)
                p.barrier()
                f.emit_ln(("dram", y2loc, relc), "ln3")
                f.emit_gate(gwT)
                f.emit_experts([w1r[e] for e in range(NE)],
                               [w3r[e] for e in range(NE)],
                               [w2r[e] for e in range(NE)], w1s, w3s, w2s)
                f.emit_resid(y2loc, relc, 0.5, out_fm_ap=y3_d)
            f.s_out.wait(sy)
            p.barrier()

        p.begin_arms()
        for i in nc.Switch(engines=engines,
                           index=[engines[j].snap(regs[nm])
                                  for j, nm in enumerate(["sy", "g", "t", "v", "a"])],
                           n=2):
            emit_body(i)
            p.end_arm()
        p.end_arms()
    return nc


def kernel_fused(L0_attn_hidden_states, L0_attn_w_qkv, L0_attn_w_o,
                 L0_ffn_gate_w, L0_ffn_gate_b, L0_ffn_w1_shared,
                 L0_ffn_w2_shared, L0_ffn_w3_shared, L0_ffn_w1_routed,
                 L0_ffn_w2_routed, L0_ffn_w3_routed, sa=5120, seq=SEQ):
    x = np.ascontiguousarray(np.asarray(L0_attn_hidden_states, dtype=np.float32))
    flat = x.reshape(B * seq, D)
    w1s = host_pieces(np.asarray(L0_ffn_w1_shared, np.float32), DC, HC)
    w3s = host_pieces(np.asarray(L0_ffn_w3_shared, np.float32), DC, HC)
    w2s = host_pieces(np.asarray(L0_ffn_w2_shared, np.float32), HC, DC)
    w1r = np.stack([host_pieces(np.asarray(L0_ffn_w1_routed[e], np.float32),
                                DC, HC) for e in range(NE)])
    w3r = np.stack([host_pieces(np.asarray(L0_ffn_w3_routed[e], np.float32),
                                DC, HC) for e in range(NE)])
    w2r = np.stack([host_pieces(np.asarray(L0_ffn_w2_routed[e], np.float32),
                                HC, DC) for e in range(NE)])
    gwT = np.ascontiguousarray(np.asarray(L0_ffn_gate_w, np.float32).T
                               .reshape(DC, 128, 16).transpose(1, 0, 2))
    gb = np.ascontiguousarray(np.asarray(L0_ffn_gate_b, np.float32)[None, :])
    wqkvT = host_pieces(np.ascontiguousarray(
        np.asarray(L0_attn_w_qkv, np.float32).T), DC, 3 * DC)
    woT = host_pieces(np.ascontiguousarray(
        np.asarray(L0_attn_w_o, np.float32).T), DC, DC)

    ck = f"fused{seq}_{sa}"
    if ck not in _cache:
        _cache[ck] = build_fused(seq, sa)
    nc = _cache[ck]
    base = {"w1s": w1s, "w3s": w3s, "w2s": w2s, "w1r": w1r, "w3r": w3r,
            "w2r": w2r, "gwT": gwT, "gb": gb, "wqkvT": wqkvT, "woT": woT}
    yt1 = seq // 2
    xt1 = sa // 2
    ins = []
    for cidx in range(NCORES):
        pairi = cidx // 2
        mem = cidx % 2
        role = pairi % 2
        b = pairi // 2
        if role == 0:
            tok0 = b * seq + mem * xt1
            xloc = np.zeros((D, yt1), np.float32)
            xloc[:, 0:xt1] = flat[tok0:tok0 + xt1].T
        else:
            tok0 = b * seq + mem * yt1
            xloc = np.ascontiguousarray(flat[tok0:tok0 + yt1].T)
        ins.append(dict(base, rid=np.array([[role, mem, 0, 0]], np.int32),
                        x=xloc))
    res = run_bass_kernel_spmd(nc, ins, core_ids=list(range(NCORES)))
    out = np.empty((B * seq, D), np.float32)
    xt2 = sa // 2
    yt2 = (seq - sa) // 2
    for cidx in range(NCORES):
        pairi = cidx // 2
        mem = cidx % 2
        role = pairi % 2
        b = pairi // 2
        y3 = np.asarray(res.results[cidx]["y3"])
        if role == 0:
            t0 = b * seq + mem * xt2
            out[t0:t0 + xt2] = y3[:, 0:xt2].T
        else:
            t0 = b * seq + sa + mem * yt2
            out[t0:t0 + yt2] = y3[:, 0:yt2].T
    return np.ascontiguousarray(out.reshape(B, seq, D))
